# version 01
# baseline (speedup 1.0000x reference)
import contextlib
import numpy as np
import concourse.bass as bass
import concourse.mybir as mybir
from concourse.bass_utils import run_bass_kernel_spmd

F32 = mybir.dt.float32
BF16 = mybir.dt.bfloat16
U32 = mybir.dt.uint32
AF = mybir.ActivationFunctionType
ALU = mybir.AluOpType

H = 1024
T = 2048
E = 8
HD = 64
NHC = H // 128

_CFG = {"n_seq": 2, "sub": ("f", "a", "f")}
_LAST = None


class Sem:
    def __init__(self, es, nc, name):
        self.h = es.enter_context(nc.semaphore(name))
        self.n = 0

    def inc(self, instr, k=1):
        instr.then_inc(self.h, k)
        self.n += k
        return self.n


def build_program(n_seq, sub):
    nc = bass.Bass("TRN2", target_bir_lowering=False)

    d_x = nc.dram_tensor("x_t", [n_seq, H, T], F32, kind="ExternalInput").ap()
    d_y = nc.dram_tensor("y_t", [n_seq, H, T], F32, kind="ExternalOutput").ap()
    d_w1 = nc.dram_tensor("W1", [E, H, H], F32, kind="ExternalInput").ap()
    d_w2 = nc.dram_tensor("W2", [E, H, H], F32, kind="ExternalInput").ap()
    d_b1c = nc.dram_tensor("b1c", [128, 64], F32, kind="ExternalInput").ap()
    d_b2 = nc.dram_tensor("b2", [E, H], F32, kind="ExternalInput").ap()
    d_wgc = nc.dram_tensor("wgc", [128, 64], F32, kind="ExternalInput").ap()
    d_bgc = nc.dram_tensor("bgc", [8, 1], F32, kind="ExternalInput").ap()
    d_wq = nc.dram_tensor("wqT", [H, H], F32, kind="ExternalInput").ap()
    d_wk = nc.dram_tensor("wkT", [H, H], F32, kind="ExternalInput").ap()
    d_wv = nc.dram_tensor("wvT", [H, H], F32, kind="ExternalInput").ap()
    d_wo = nc.dram_tensor("woT", [H, H], F32, kind="ExternalInput").ap()
    d_rf = nc.dram_tensor("rfdup", [128, HD], F32, kind="ExternalInput").ap()
    d_boc = nc.dram_tensor("boc", [128, 8], F32, kind="ExternalInput").ap()
    d_ident = nc.dram_tensor("ident", [128, 128], F32, kind="ExternalInput").ap()
    d_iota8 = nc.dram_tensor("iota8", [128, 8], F32, kind="ExternalInput").ap()
    d_negsel = nc.dram_tensor("negsel", [2, 128], F32, kind="ExternalInput").ap()
    d_hsel = nc.dram_tensor("hsel", [16, H], F32, kind="ExternalInput").ap()
    d_esel = nc.dram_tensor("esel", [8, H], F32, kind="ExternalInput").ap()

    dbg = _CFG.get("dbg", False)
    if dbg:
        d_dbg_xn = nc.dram_tensor("dbg_xn", [H, T], BF16, kind="ExternalOutput").ap()
        d_dbg_lg = nc.dram_tensor("dbg_lg", [8, T], F32, kind="ExternalOutput").ap()
        d_dbg_oh = nc.dram_tensor("dbg_oh", [8, T], BF16, kind="ExternalOutput").ap()
        d_dbg_h1 = nc.dram_tensor("dbg_h1", [128, 8192], BF16, kind="ExternalOutput").ap()
        d_dbg_mk = nc.dram_tensor("dbg_mk", [128, T], BF16, kind="ExternalOutput").ap()
        d_dbg_xe = nc.dram_tensor("dbg_xe", [E, H, T], F32, kind="ExternalOutput").ap()
        d_dbg_v = nc.dram_tensor("dbg_v", [128, 16384], BF16, kind="ExternalOutput").ap()
        d_dbg_kv = nc.dram_tensor("dbg_kv", [128, 520], BF16, kind="ExternalOutput").ap()
        d_dbg_ar = nc.dram_tensor("dbg_ar", [128, 16384], BF16, kind="ExternalOutput").ap()
        d_dbg_nr = nc.dram_tensor("dbg_nr", [16, T], BF16, kind="ExternalOutput").ap()
    es = contextlib.ExitStack()
    with es:
        sbt = lambda name, shape, dt: es.enter_context(nc.sbuf_tensor(name, shape, dt))
        pst = lambda name, shape: es.enter_context(nc.psum_tensor(name, shape, F32))

        x = [sbt(f"x{i}", [128, T], F32) for i in range(NHC)]
        xn = [sbt(f"xn{i}", [128, T], BF16) for i in range(NHC)]
        w1s = sbt("w1s", [128, 8192], BF16)
        w2s = sbt("w2s", [128, 8192], BF16)
        h1m = sbt("h1m", [128, 8192], BF16)
        vat = sbt("vat", [128, 16384], BF16)
        ar32 = sbt("ar32", [128, 3072], F32)
        ident = sbt("sb_ident", [128, 128], F32)
        iota8 = sbt("sb_iota8", [128, 8], F32)
        negsel = sbt("sb_negsel", [2, 128], BF16)
        hsel = sbt("sb_hsel", [16, H], BF16)
        esel = sbt("sb_esel", [8, H], BF16)
        rf_f = sbt("rf_f", [128, HD], F32)
        rf_s = sbt("rf_s", [128, HD], BF16)
        b1c = sbt("sb_b1c", [128, 64], F32)
        b2bf = sbt("b2bf", [8, H], BF16)
        wg = sbt("wg", [128, 64], BF16)
        bgc = sbt("sb_bgc", [8, 1], F32)
        boc = sbt("sb_boc", [128, 8], F32)
        ones_col = sbt("ones_col", [128, 1], BF16)
        ones_bc = sbt("ones_bc", [1, 128], BF16)
        bdiag = sbt("bdiag", [128, 2], BF16)
        eps_col = sbt("eps_col", [1, 1], F32)
        kvext = sbt("kvext", [128, 520], BF16)
        mx8 = sbt("mx8", [128, 8], F32)
        idxu = sbt("idxu", [128, 8], U32)
        idxf = sbt("idxf", [128, 1], F32)
        ohf = sbt("ohf", [128, 128], F32)
        ohstage = sbt("ohstage", [128, 8], F32)
        nrm16 = sbt("nrm16", [16, T], BF16)
        tmp_r = [sbt("tmp_r0", [1, 1024], BF16), sbt("tmp_r1", [1, 1024], BF16)]

        h1t = lambda ic: h1m[:, ic * 1024:(ic + 1) * 1024]
        xb_sl = [h1m[:, 0:2048], h1m[:, 2048:4096]]
        xsq_sl = [h1m[:, 4096:6144], h1m[:, 6144:8192]]
        kq_sb = h1m[:, 0:2048]
        sq_sb = h1m[:, 2048:4096]
        kp_sb = h1m[:, 4096:5120]
        nr_bf = h1m[0:2, 5120:7168]
        eq_sb = h1m[:, 2048:4096]
        mask_sl = [vat[:, 0:2048], vat[:, 2048:4096]]
        onehotT = vat[0:8, 4096:6144]
        m_bf = vat[0:1, 6144:7168]
        rstd_bf = vat[0:1, 7168:8192]
        vtile = lambda tt: vat[:, tt * 1024:(tt + 1) * 1024]
        araw = lambda oc: vat[:, oc * 2048:(oc + 1) * 2048]
        t1_sl = ar32[:, 0:1024]
        m_row = ar32[0:1, 1024:2048]
        pair_f = ar32[0:2, 1024:2048]
        msq_row = ar32[0:1, 2048:3072]
        var_row = ar32[0:1, 0:1024]
        rstd_f = ar32[0:1, 2048:3072]
        lg_sb = ar32[0:8, 0:2048]

        PG = [pst(f"PG{i}", [128, 1024]) for i in range(4)]

        dl = Sem(es, nc, "dl")
        xl = Sem(es, nc, "xl")
        pe = Sem(es, nc, "pe")
        act = Sem(es, nc, "act")
        dve = Sem(es, nc, "dve")
        sp = Sem(es, nc, "sp")
        st = Sem(es, nc, "st")

        block = es.enter_context(nc.Block())

        @block.gpsimd
        def _(g):
            for dst, src in [(ident[:], d_ident), (iota8[:], d_iota8),
                             (negsel[:], d_negsel), (hsel[:], d_hsel),
                             (esel[:], d_esel), (rf_f[:], d_rf),
                             (b1c[:], d_b1c), (b2bf[:], d_b2),
                             (wg[:], d_wgc), (bgc[:], d_bgc), (boc[:], d_boc)]:
                dl.inc(g.dma_start(dst, src), 16)
            dl.inc(g.memset(ones_col[:], 1.0))
            dl.inc(g.memset(ones_bc[:], 1.0))
            dl.inc(g.memset(bdiag[:], 0.0))
            dl.inc(g.memset(bdiag[0:64, 0:1], 1.0))
            dl.inc(g.memset(bdiag[64:128, 1:2], 1.0))
            dl.inc(g.memset(eps_col[:], 1e-5))
        init_dl = dl.n

        @block.scalar
        def _(s):
            s.wait_ge(dl.h, init_dl)
            act.inc(s.activation(rf_s[:], rf_f[:], AF.Copy, scale=0.125))

        def vstep(v, instr):
            dve.inc(instr)
            v.wait_ge(dve.h, dve.n)

        def w_src(d_w):
            return d_w.rearrange("(c p) i -> p c i", p=128)

        def w_dst(slot):
            return slot[:].rearrange("p (c i) -> p c i", c=8)

        def emit_ln(x_ready):
            cast_acts, cast_dves, pe_marks = [], [], []
            for hc in range(NHC):
                sl = hc % 2

                @block.scalar
                def _(s, hc=hc, sl=sl):
                    if hc == 0:
                        for sem, val in x_ready:
                            s.wait_ge(sem, val)
                    if hc >= 2:
                        s.wait_ge(pe.h, pe_marks[hc - 2])
                    act.inc(s.activation(xb_sl[sl], x[hc][:], AF.Copy))
                cast_acts.append(act.n)

                @block.vector
                def _(v, hc=hc, sl=sl):
                    if hc == 0:
                        for sem, val in x_ready:
                            v.wait_ge(sem, val)
                    if hc >= 2:
                        v.wait_ge(pe.h, pe_marks[hc - 2])
                    dve.inc(v.tensor_mul(xsq_sl[sl], x[hc][:], x[hc][:]))
                cast_dves.append(dve.n)

                @block.tensor
                def _(t, hc=hc, sl=sl):
                    t.wait_ge(act.h, cast_acts[hc])
                    t.wait_ge(dve.h, cast_dves[hc])
                    first = hc == 0
                    last = hc == NHC - 1
                    for half in range(2):
                        pgs, pgq = (PG[0], PG[1]) if half == 0 else (PG[2], PG[3])
                        for c in range(2):
                            co = half * 1024 + c * 512
                            t.matmul(pgs[0:1, c * 512:(c + 1) * 512],
                                     ones_col[:], xb_sl[sl][:, co:co + 512],
                                     start=first, stop=last)
                            i2 = t.matmul(pgq[0:1, c * 512:(c + 1) * 512],
                                          ones_col[:], xsq_sl[sl][:, co:co + 512],
                                          start=first, stop=last)
                    pe.inc(i2)
                pe_marks.append(pe.n)
            sums_pe = pe.n

            xn_mark = 0
            for half in range(2):
                pgs, pgq = (PG[0], PG[1]) if half == 0 else (PG[2], PG[3])

                @block.vector
                def _(v, pgs=pgs, pgq=pgq):
                    v.wait_ge(pe.h, sums_pe)
                    vstep(v, v.tensor_scalar_mul(m_row, pgs[0:1, :], 1.0 / H))
                    vstep(v, v.tensor_scalar_mul(msq_row, pgq[0:1, :], 1.0 / H))
                    vstep(v, v.tensor_mul(var_row, m_row, m_row))
                    dve.inc(v.tensor_sub(var_row, msq_row, var_row))
                v_rows = dve.n

                @block.scalar
                def _(s):
                    s.wait_ge(dve.h, v_rows)
                    act.inc(s.activation(rstd_f, var_row, AF.Ln,
                                         bias=eps_col[0:1, :]))
                    s.wait_ge(act.h, act.n)
                    act.inc(s.activation(rstd_f, rstd_f, AF.Exp, scale=-0.5))
                    s.wait_ge(act.h, act.n)
                    s.activation(m_bf[0:1, 0:1024], m_row, AF.Copy)
                    act.inc(s.activation(rstd_bf[0:1, 0:1024], rstd_f, AF.Copy))
                s_rows = act.n

                @block.tensor
                def _(t, pgs=pgs, pgq=pgq):
                    t.wait_ge(act.h, s_rows)
                    for c in range(2):
                        t.matmul(pgs[:, c * 512:(c + 1) * 512], ones_bc[:],
                                 m_bf[0:1, c * 512:(c + 1) * 512],
                                 start=True, stop=True)
                        i = t.matmul(pgq[:, c * 512:(c + 1) * 512], ones_bc[:],
                                     rstd_bf[0:1, c * 512:(c + 1) * 512],
                                     start=True, stop=True)
                    pe.inc(i)
                bc_pe = pe.n

                @block.vector
                def _(v, half=half, pgs=pgs, pgq=pgq):
                    v.wait_ge(pe.h, bc_pe)
                    co = half * 1024
                    for hc in range(NHC):
                        vstep(v, v.scalar_tensor_tensor(
                            t1_sl, x[hc][:, co:co + 1024], 1.0, pgs[:],
                            ALU.mult, ALU.subtract))
                        i = v.tensor_mul(xn[hc][:, co:co + 1024], t1_sl, pgq[:])
                        if hc < NHC - 1:
                            vstep(v, i)
                    dve.inc(i)
                xn_mark = dve.n
            return xn_mark

        def emit_gate(xn_ready):
            @block.tensor
            def _(t):
                t.wait_ge(dve.h, xn_ready)
                t.wait_ge(dl.h, init_dl)
                for half in range(2):
                    pg = PG[half]
                    for hc in range(NHC):
                        for c in range(2):
                            co = half * 1024 + c * 512
                            i = t.matmul(pg[0:8, c * 512:(c + 1) * 512],
                                         wg[:, hc * 8:hc * 8 + 8],
                                         xn[hc][:, co:co + 512],
                                         start=(hc == 0), stop=(hc == NHC - 1))
                pe.inc(i)
            g_pe = pe.n

            @block.scalar
            def _(s):
                s.wait_ge(pe.h, g_pe)
                for half in range(2):
                    i = s.activation(lg_sb[:, half * 1024:(half + 1) * 1024],
                                     PG[half][0:8, :], AF.Identity, bias=bgc[:])
                act.inc(i)
            lg_act = act.n

            @block.tensor
            def _(t):
                t.wait_ge(act.h, lg_act)
                for tt in range(16):
                    i = t.transpose(PG[2][:, tt * 8:tt * 8 + 8],
                                    lg_sb[:, tt * 128:(tt + 1) * 128],
                                    ident[0:8, 0:8])
                pe.inc(i)
            tr_pe = pe.n

            @block.vector
            def _(v):
                v.wait_ge(pe.h, tr_pe)
                for tt in range(16):
                    vstep(v, v.tensor_copy(ohstage[:], PG[2][:, tt * 8:tt * 8 + 8]))
                    vstep(v, v.max(mx8[:], ohstage[:]))
                    vstep(v, v.max_index(idxu[:], mx8[:], ohstage[:]))
                    vstep(v, v.tensor_copy(idxf[:], idxu[:, 0:1]))
                    i = v.tensor_scalar(ohf[:, tt * 8:tt * 8 + 8], iota8[:],
                                        idxf[:], None, ALU.is_equal)
                    if tt < 15:
                        vstep(v, i)
                dve.inc(i)
            oh_dve = dve.n

            ohT_pes, ohT_acts = [], []
            for tt in range(16):
                off = (tt % 2) * 512

                @block.tensor
                def _(t, tt=tt, off=off):
                    if tt == 0:
                        t.wait_ge(dve.h, oh_dve)
                    if tt >= 2:
                        t.wait_ge(act.h, ohT_acts[tt - 2])
                    i = t.transpose(PG[3][0:8, off:off + 128],
                                    ohf[:, tt * 8:tt * 8 + 8], ident[:])
                    pe.inc(i)
                ohT_pes.append(pe.n)

                @block.scalar
                def _(s, tt=tt, off=off):
                    s.wait_ge(pe.h, ohT_pes[tt])
                    i = s.copy(onehotT[:, tt * 128:(tt + 1) * 128],
                               PG[3][0:8, off:off + 128])
                    act.inc(i)
                ohT_acts.append(act.n)
            return act.n

        def emit_mask(e_idx, slot, oh_act, war_act):
            @block.tensor
            def _(t):
                t.wait_ge(act.h, max(oh_act, war_act))
                for half in range(2):
                    for c in range(2):
                        co = half * 1024 + c * 512
                        i = t.matmul(PG[half][:, c * 512:(c + 1) * 512],
                                     esel[:, e_idx * 128:(e_idx + 1) * 128],
                                     onehotT[:, co:co + 512],
                                     start=True, stop=True)
                pe.inc(i)
            mpe = pe.n

            @block.scalar
            def _(s):
                s.wait_ge(pe.h, mpe)
                for half in range(2):
                    i = s.copy(mask_sl[slot][:, half * 1024:(half + 1) * 1024],
                               PG[half][:])
                act.inc(i)
            return act.n

        def emit_ffn(x_ready, w_marks, rearm):
            xn_ready = emit_ln(x_ready)
            if dbg and not hasattr(emit_ffn, "_dumped_ln"):
                emit_ffn._dumped_ln = True

                @block.sync
                def _(sy, xr=xn_ready):
                    sy.wait_ge(dve.h, xr)
                    for hc in range(NHC):
                        st.inc(sy.dma_start(
                            d_dbg_xn[hc * 128:(hc + 1) * 128, :], xn[hc][:]), 16)
            oh_act = emit_gate(xn_ready)
            if dbg and not hasattr(emit_ffn, "_dumped_gate"):
                emit_ffn._dumped_gate = True

                @block.sync
                def _(sy, oa=oh_act):
                    sy.wait_ge(act.h, oa)
                    st.inc(sy.dma_start(d_dbg_lg, lg_sb), 16)
                    st.inc(sy.dma_start(d_dbg_oh, onehotT), 16)
            mask_acts = {0: emit_mask(0, 0, oh_act, war_act=oh_act)}
            last_ps1_act = mask_acts[0]
            acc_dves = []
            w1_free_pe = 0
            for e in range(E):
                gelu_acts = []
                maskd_dves = []
                for tb in range(2):
                    for ic in range(8):
                        pg = PG[ic % 2]

                        @block.tensor
                        def _(t, e=e, tb=tb, ic=ic, pg=pg,
                              k=len(gelu_acts), lpa=last_ps1_act):
                            if tb == 0 and ic == 0:
                                t.wait_ge(dl.h, w_marks[("w1", e)])
                                if e == 0:
                                    t.wait_ge(dve.h, xn_ready)
                            if k >= 2:
                                t.wait_ge(act.h, gelu_acts[k - 2])
                            else:
                                t.wait_ge(act.h, lpa)
                            for hc in range(NHC):
                                for c in range(2):
                                    i = t.matmul(
                                        pg[:, c * 512:(c + 1) * 512],
                                        w1s[:, hc * 1024 + ic * 128:
                                            hc * 1024 + (ic + 1) * 128],
                                        xn[hc][:, tb * 1024 + c * 512:
                                               tb * 1024 + (c + 1) * 512],
                                        start=(hc == 0), stop=(hc == NHC - 1))
                            pe.inc(i)
                        mm1_pe = pe.n
                        if tb == 1 and ic == 7:
                            w1_free_pe = pe.n

                        @block.scalar
                        def _(s, e=e, ic=ic, pg=pg, mp=mm1_pe):
                            s.wait_ge(pe.h, mp)
                            i = s.activation(h1t(ic), pg[:], AF.Gelu,
                                             bias=b1c[:, e * 8 + ic:
                                                      e * 8 + ic + 1])
                            act.inc(i)
                        gelu_acts.append(act.n)

                        @block.vector
                        def _(v, e=e, tb=tb, ic=ic, ga=gelu_acts[-1],
                              ma=mask_acts[e]):
                            v.wait_ge(act.h, max(ga, ma))
                            i = v.tensor_mul(
                                h1t(ic), h1t(ic),
                                mask_sl[e % 2][:, tb * 1024:(tb + 1) * 1024])
                            dve.inc(i)
                        maskd_dves.append(dve.n)
                    if dbg and e == 2 and tb == 0 and not hasattr(emit_ffn, "_dumped_h1"):
                        emit_ffn._dumped_h1 = True

                        @block.sync
                        def _(sy, md=maskd_dves[7]):
                            sy.wait_ge(dve.h, md)
                            st.inc(sy.dma_start(d_dbg_h1, h1m[:]), 16)

                    if tb == 1 and e + 1 < E:
                        mask_acts[e + 1] = emit_mask(
                            e + 1, (e + 1) % 2, oh_act, war_act=gelu_acts[-1])
                        last_ps1_act = mask_acts[e + 1]
                        if dbg and e + 1 == 2:
                            @block.sync
                            def _(sy, ma=mask_acts[2]):
                                sy.wait_ge(act.h, ma)
                                st.inc(sy.dma_start(d_dbg_mk, mask_sl[0][:]), 16)

                    for hc in range(NHC):
                        pg2 = PG[2 + hc % 2]

                        @block.tensor
                        def _(t, e=e, tb=tb, hc=hc, pg2=pg2,
                              md=maskd_dves[tb * 8 + 7], k=len(acc_dves)):
                            if tb == 0 and hc == 0:
                                t.wait_ge(dl.h, w_marks[("w2", e)])
                            t.wait_ge(dve.h, md)
                            if dbg:
                                t.wait_ge(st.h, st.n)
                            if k >= 2:
                                t.wait_ge(dve.h, acc_dves[k - 2])
                            first = True
                            if e == 0:
                                for c in range(2):
                                    t.matmul(pg2[:, c * 512:(c + 1) * 512],
                                             b2bf[:, hc * 128:(hc + 1) * 128],
                                             onehotT[:, tb * 1024 + c * 512:
                                                     tb * 1024 + (c + 1) * 512],
                                             start=True, stop=False)
                                first = False
                            for ic in range(8):
                                for c in range(2):
                                    i = t.matmul(
                                        pg2[:, c * 512:(c + 1) * 512],
                                        w2s[:, ic * 1024 + hc * 128:
                                            ic * 1024 + (hc + 1) * 128],
                                        h1t(ic)[:, c * 512:(c + 1) * 512],
                                        start=(first and ic == 0),
                                        stop=(ic == 7))
                            pe.inc(i)
                        mm2_pe = pe.n

                        @block.vector
                        def _(v, tb=tb, hc=hc, pg2=pg2, mp=mm2_pe, stn=st.n):
                            v.wait_ge(pe.h, mp)
                            if dbg:
                                v.wait_ge(st.h, stn)
                            i = v.scalar_tensor_tensor(
                                x[hc][:, tb * 1024:(tb + 1) * 1024],
                                pg2[:], 0.5,
                                x[hc][:, tb * 1024:(tb + 1) * 1024],
                                ALU.mult, ALU.add)
                            dve.inc(i)
                        acc_dves.append(dve.n)

                if dbg and not hasattr(emit_ffn, f"_dx{e}"):
                    setattr(emit_ffn, f"_dx{e}", True)

                    @block.sync
                    def _(sy, e=e, ad=acc_dves[-1]):
                        sy.wait_ge(dve.h, ad)
                        for hc in range(NHC):
                            st.inc(sy.dma_start(
                                d_dbg_xe[e, hc * 128:(hc + 1) * 128, :],
                                x[hc][:]), 16)

                if e + 1 < E:
                    @block.gpsimd
                    def _(g, e=e, wf=w1_free_pe):
                        g.wait_ge(pe.h, wf)
                        w_marks[("w1", e + 1)] = dl.inc(
                            g.dma_start(w_dst(w1s), w_src(d_w1[e + 1])), 16)
                        g.wait_ge(pe.h, pe.n)
                        w_marks[("w2", e + 1)] = dl.inc(
                            g.dma_start(w_dst(w2s), w_src(d_w2[e + 1])), 16)
            if rearm is not None:
                d1, d2 = rearm

                @block.gpsimd
                def _(g):
                    g.wait_ge(pe.h, pe.n)
                    w_marks[("w1", 0)] = dl.inc(
                        g.dma_start(w_dst(w1s), w_src(d1)), 16)
                    w_marks[("w2", 0)] = dl.inc(
                        g.dma_start(w_dst(w2s), w_src(d2)), 16)
            return dve.n

        def emit_attn(x_ready, w_marks):
            astage = _CFG.get("astage", 4)
            xn_ready = emit_ln(x_ready)

            v_acts = []
            for tt in range(16):
                pg = PG[tt % 4]

                @block.tensor
                def _(t, tt=tt, pg=pg):
                    if tt == 0:
                        t.wait_ge(dl.h, w_marks["wv"])
                        t.wait_ge(dve.h, xn_ready)
                    if tt >= 4:
                        t.wait_ge(act.h, v_acts[tt - 4])
                    for hc in range(NHC):
                        for c in range(2):
                            i = t.matmul(
                                pg[:, c * 512:(c + 1) * 512],
                                xn[hc][:, tt * 128:(tt + 1) * 128],
                                w1s[:, hc * 1024 + c * 512:
                                    hc * 1024 + (c + 1) * 512],
                                start=(hc == 0), stop=(hc == NHC - 1))
                    pe.inc(i)
                v_pe = pe.n

                @block.scalar
                def _(s, tt=tt, pg=pg, vp=v_pe):
                    s.wait_ge(pe.h, vp)
                    act.inc(s.copy(vtile(tt), pg[:]))
                v_acts.append(act.n)
            v_done_pe = pe.n
            if dbg:
                @block.sync
                def _(sy, va=v_acts[-1]):
                    sy.wait_ge(act.h, va)
                    st.inc(sy.dma_start(d_dbg_v, vat[:]), 16)
            if astage <= 1:
                return xn_ready

            @block.gpsimd
            def _(g, wf=v_done_pe):
                g.wait_ge(pe.h, wf)
                w_marks["wq"] = dl.inc(g.dma_start(w_dst(w1s), w_src(d_wq)), 16)

            prev_evac_act = v_acts[-1]
            prev_proj_pe = 0
            for oc in range(8):
                @block.tensor
                def _(t, oc=oc, pea=prev_evac_act):
                    if oc == 0:
                        t.wait_ge(dl.h, w_marks["wk"])
                    t.wait_ge(act.h, pea)
                    for hc in range(NHC):
                        for half in range(2):
                            for c in range(2):
                                i = t.matmul(
                                    PG[half][:, c * 512:(c + 1) * 512],
                                    w2s[:, hc * 1024 + oc * 128:
                                        hc * 1024 + (oc + 1) * 128],
                                    xn[hc][:, half * 1024 + c * 512:
                                           half * 1024 + (c + 1) * 512],
                                    start=(hc == 0), stop=(hc == NHC - 1))
                    pe.inc(i)
                qk_pe = pe.n
                if oc == 7:
                    @block.gpsimd
                    def _(g, wf=qk_pe):
                        g.wait_ge(pe.h, wf)
                        w_marks["wo"] = dl.inc(
                            g.dma_start(w_dst(w2s), w_src(d_wo)), 16)

                @block.scalar
                def _(s, qp=qk_pe, pp=prev_proj_pe):
                    s.wait_ge(pe.h, max(qp, pp))
                    s.copy(kq_sb[:, 0:1024], PG[0][:])
                    s.copy(kq_sb[:, 1024:2048], PG[1][:])
                    s.activation(sq_sb[:, 0:1024], PG[0][:], AF.Square)
                    i = s.activation(sq_sb[:, 1024:2048], PG[1][:], AF.Square)
                    act.inc(i)
                evac_act = act.n

                @block.tensor
                def _(t, ea=evac_act):
                    t.wait_ge(act.h, ea)
                    for half in range(2):
                        pgn = PG[2] if half == 0 else PG[3]
                        for c in range(2):
                            i = t.matmul(pgn[0:2, c * 512:(c + 1) * 512],
                                         bdiag[:],
                                         sq_sb[:, half * 1024 + c * 512:
                                               half * 1024 + (c + 1) * 512],
                                         start=True, stop=True)
                    pe.inc(i)
                nrm_pe = pe.n

                @block.scalar
                def _(s, np_=nrm_pe):
                    s.wait_ge(pe.h, np_)
                    for half in range(2):
                        pgn = PG[2] if half == 0 else PG[3]
                        act.inc(s.activation(pair_f, pgn[0:2, :], AF.Ln))
                        s.wait_ge(act.h, act.n)
                        i = s.activation(
                            nr_bf[0:2, half * 1024:(half + 1) * 1024],
                            pair_f, AF.Exp, scale=0.5)
                        if half == 0:
                            act.inc(i)
                            s.wait_ge(act.h, act.n)
                    act.inc(i)
                nr_act = act.n

                for hh in range(2):
                    @block.tensor
                    def _(t, hh=hh, na=nr_act, pea=prev_evac_act):
                        t.wait_ge(act.h, na)
                        t.wait_ge(act.h, pea)
                        for tt in range(16):
                            v_ = PG[2][:, tt * 64:(tt + 1) * 64]
                            t.matmul(v_,
                                     kq_sb[hh * 64:(hh + 1) * 64,
                                           tt * 128:(tt + 1) * 128],
                                     rf_s[hh * 64:(hh + 1) * 64, :],
                                     start=True, stop=False)
                            i = t.matmul(v_,
                                         nr_bf[0:2, tt * 128:(tt + 1) * 128],
                                         negsel[0:2, hh * 64:(hh + 1) * 64],
                                         start=False, stop=True)
                        pe.inc(i)
                    proj_pe = pe.n
                    prev_proj_pe = proj_pe

                    @block.scalar
                    def _(s, pp=proj_pe):
                        s.wait_ge(pe.h, pp)
                        act.inc(s.activation(kp_sb, PG[2][:], AF.Exp))
                    kp_act = act.n

                    @block.tensor
                    def _(t, oc=oc, hh=hh, ka=kp_act):
                        t.wait_ge(act.h, ka)
                        h_ = 2 * oc + hh
                        base = hh * 512
                        for tt in range(16):
                            t.matmul(PG[3][0:64, base:base + 64],
                                     kp_sb[:, tt * 64:(tt + 1) * 64],
                                     vtile(tt)[:, h_ * 64:h_ * 64 + 64],
                                     start=(tt == 0), stop=(tt == 15))
                            i = t.matmul(PG[2][0:64, 960 + hh:960 + hh + 1],
                                         kp_sb[:, tt * 64:(tt + 1) * 64],
                                         ones_col[:],
                                         start=(tt == 0), stop=(tt == 15))
                        pe.inc(i)
                    kv_pe = pe.n

                    @block.scalar
                    def _(s, oc=oc, hh=hh, kp=kv_pe, base=hh * 512):
                        s.wait_ge(pe.h, kp)
                        s.copy(kvext[hh * 64:(hh + 1) * 64,
                                     oc * 65:oc * 65 + 64],
                               PG[3][0:64, base:base + 64])
                        i = s.copy(kvext[hh * 64:(hh + 1) * 64,
                                         oc * 65 + 64:oc * 65 + 65],
                                   PG[2][0:64, 960 + hh:960 + hh + 1])
                        act.inc(i)
                    prev_evac_act = act.n

            if dbg:
                @block.sync
                def _(sy, ka=prev_evac_act):
                    sy.wait_ge(act.h, ka)
                    st.inc(sy.dma_start(d_dbg_kv, kvext[:]), 16)
            if astage <= 2:
                return xn_ready

            prev_q_evac = prev_evac_act
            sp_marks = [0, 0]
            for oc in range(8):
                @block.tensor
                def _(t, oc=oc, pqe=prev_q_evac):
                    if oc == 0:
                        t.wait_ge(dl.h, w_marks["wq"])
                    t.wait_ge(act.h, pqe)
                    for hc in range(NHC):
                        for half in range(2):
                            for c in range(2):
                                i = t.matmul(
                                    PG[half][:, c * 512:(c + 1) * 512],
                                    w1s[:, hc * 1024 + oc * 128:
                                        hc * 1024 + (oc + 1) * 128],
                                    xn[hc][:, half * 1024 + c * 512:
                                           half * 1024 + (c + 1) * 512],
                                    start=(hc == 0), stop=(hc == NHC - 1))
                    pe.inc(i)
                qk_pe = pe.n

                @block.scalar
                def _(s, qp=qk_pe, pp=prev_proj_pe):
                    s.wait_ge(pe.h, max(qp, pp))
                    s.copy(kq_sb[:, 0:1024], PG[0][:])
                    s.copy(kq_sb[:, 1024:2048], PG[1][:])
                    s.activation(sq_sb[:, 0:1024], PG[0][:], AF.Square)
                    i = s.activation(sq_sb[:, 1024:2048], PG[1][:], AF.Square)
                    act.inc(i)
                evac_act = act.n

                @block.tensor
                def _(t, ea=evac_act):
                    t.wait_ge(act.h, ea)
                    for half in range(2):
                        pgn = PG[2] if half == 0 else PG[3]
                        for c in range(2):
                            i = t.matmul(pgn[0:2, c * 512:(c + 1) * 512],
                                         bdiag[:],
                                         sq_sb[:, half * 1024 + c * 512:
                                               half * 1024 + (c + 1) * 512],
                                         start=True, stop=True)
                    pe.inc(i)
                nrm_pe = pe.n

                @block.scalar
                def _(s, np_=nrm_pe):
                    s.wait_ge(pe.h, np_)
                    for half in range(2):
                        pgn = PG[2] if half == 0 else PG[3]
                        act.inc(s.activation(pair_f, pgn[0:2, :], AF.Ln))
                        s.wait_ge(act.h, act.n)
                        i = s.activation(
                            nr_bf[0:2, half * 1024:(half + 1) * 1024],
                            pair_f, AF.Exp, scale=0.5)
                        if half == 0:
                            act.inc(i)
                            s.wait_ge(act.h, act.n)
                    act.inc(i)
                nr_act = act.n

                @block.tensor
                def _(t, na=nr_act):
                    t.wait_ge(act.h, na)
                    for hh in range(2):
                        for half in range(2):
                            pgn = PG[2] if half == 0 else PG[3]
                            for c in range(2):
                                v_ = pgn[hh * 64:(hh + 1) * 64,
                                         c * 512:(c + 1) * 512]
                                t.matmul(v_, rf_s[hh * 64:(hh + 1) * 64, :],
                                         kq_sb[hh * 64:(hh + 1) * 64,
                                               half * 1024 + c * 512:
                                               half * 1024 + (c + 1) * 512],
                                         start=True, stop=False)
                                i = t.matmul(
                                    v_, negsel[0:2, hh * 64:(hh + 1) * 64],
                                    nr_bf[0:2, half * 1024 + c * 512:
                                          half * 1024 + (c + 1) * 512],
                                    start=False, stop=True)
                    pe.inc(i)
                proj_pe = pe.n
                prev_proj_pe = proj_pe

                @block.scalar
                def _(s, pp=proj_pe):
                    s.wait_ge(pe.h, pp)
                    s.activation(eq_sb[:, 0:1024], PG[2][:], AF.Exp)
                    i = s.activation(eq_sb[:, 1024:2048], PG[3][:], AF.Exp)
                    act.inc(i)
                eq_act = act.n

                for hh in range(2):
                    @block.tensor
                    def _(t, oc=oc, hh=hh, ea=eq_act, pqe=prev_q_evac):
                        t.wait_ge(act.h, ea)
                        t.wait_ge(act.h, pqe)
                        for half in range(2):
                            pgo = PG[0] if half == 0 else PG[1]
                            for c in range(2):
                                i = t.matmul(
                                    pgo[0:65, c * 512:(c + 1) * 512],
                                    kvext[hh * 64:(hh + 1) * 64,
                                          oc * 65:(oc + 1) * 65],
                                    eq_sb[hh * 64:(hh + 1) * 64,
                                          half * 1024 + c * 512:
                                          half * 1024 + (c + 1) * 512],
                                    start=True, stop=True)
                        pe.inc(i)
                    out_pe = pe.n

                    @block.scalar
                    def _(s, oc=oc, hh=hh, op=out_pe, spm=sp_marks[0]):
                        s.wait_ge(pe.h, op)
                        s.wait_ge(sp.h, spm)
                        for half in range(2):
                            pgo = PG[0] if half == 0 else PG[1]
                            s.copy(araw(oc)[hh * 64:(hh + 1) * 64,
                                            half * 1024:(half + 1) * 1024],
                                   pgo[0:64, :])
                            i = s.copy(tmp_r[half][0:1, :], pgo[64:65, :])
                        act.inc(i)
                    ev_act = act.n
                    prev_q_evac = ev_act

                    @block.sync
                    def _(sy, oc=oc, hh=hh, ea=ev_act):
                        sy.wait_ge(act.h, ea)
                        row = 2 * oc + hh
                        sp.inc(sy.dma_start(
                            nrm16[row:row + 1, 0:1024], tmp_r[0][:]), 16)
                        sp.inc(sy.dma_start(
                            nrm16[row:row + 1, 1024:2048], tmp_r[1][:]), 16)
                    sp_marks[0] = sp.n

            if dbg:
                @block.sync
                def _(sy, qa=prev_q_evac, spn=sp.n):
                    sy.wait_ge(act.h, qa)
                    sy.wait_ge(sp.h, spn)
                    st.inc(sy.dma_start(d_dbg_ar, vat[:]), 16)
                    st.inc(sy.dma_start(d_dbg_nr, nrm16[:]), 16)
            if astage <= 3:
                return xn_ready

            @block.vector
            def _(v, spn=sp.n):
                v.wait_ge(sp.h, spn)
                vstep(v, v.tensor_scalar_add(nrm16[:], nrm16[:], 1e-6))
                with nc.allow_low_precision(
                        reason="1/(nrm+1e-6): fp32 DVE internals, bf16 store "
                               "at matmul-operand precision"):
                    dve.inc(v.reciprocal(nrm16[:], nrm16[:]))
            al_dve = dve.n

            sc_dves = []
            for oc in range(8):
                pg_sc = (PG[2], PG[3]) if oc % 2 == 0 else (PG[0], PG[1])

                @block.tensor
                def _(t, oc=oc, ad=al_dve, pqe=prev_q_evac, pg_sc=pg_sc):
                    if oc == 0:
                        t.wait_ge(dve.h, ad)
                    if oc <= 1:
                        t.wait_ge(act.h, pqe)
                    if oc >= 2:
                        t.wait_ge(dve.h, sc_dves[oc - 2])
                    for half in range(2):
                        pgn = pg_sc[half]
                        for c in range(2):
                            i = t.matmul(pgn[:, c * 512:(c + 1) * 512],
                                         hsel[:, oc * 128:(oc + 1) * 128],
                                         nrm16[:, half * 1024 + c * 512:
                                               half * 1024 + (c + 1) * 512],
                                         start=True, stop=True)
                    pe.inc(i)
                hs_pe = pe.n

                @block.vector
                def _(v, oc=oc, hp=hs_pe, pg_sc=pg_sc):
                    v.wait_ge(pe.h, hp)
                    for half in range(2):
                        pgn = pg_sc[half]
                        i = v.tensor_mul(
                            araw(oc)[:, half * 1024:(half + 1) * 1024],
                            araw(oc)[:, half * 1024:(half + 1) * 1024],
                            pgn[:])
                    dve.inc(i)
                sc_dves.append(dve.n)

            wo_marks = []
            for oco in range(8):
                pg_pair = (PG[0], PG[1]) if oco % 2 == 0 else (PG[2], PG[3])

                @block.tensor
                def _(t, oco=oco, pg_pair=pg_pair, pqe=prev_q_evac):
                    if oco == 0:
                        t.wait_ge(dl.h, w_marks["wo"])
                        t.wait_ge(dve.h, sc_dves[-1])
                        t.wait_ge(act.h, pqe)
                    if oco >= 2:
                        t.wait_ge(dve.h, wo_marks[oco - 2])
                    for hc in range(NHC):
                        for half in range(2):
                            pgn = pg_pair[half]
                            for c in range(2):
                                i = t.matmul(
                                    pgn[:, c * 512:(c + 1) * 512],
                                    w2s[:, hc * 1024 + oco * 128:
                                        hc * 1024 + (oco + 1) * 128],
                                    araw(hc)[:, half * 1024 + c * 512:
                                             half * 1024 + (c + 1) * 512],
                                    start=(hc == 0), stop=(hc == NHC - 1))
                    pe.inc(i)
                wo_pe = pe.n

                @block.vector
                def _(v, oco=oco, wp=wo_pe, pg_pair=pg_pair):
                    v.wait_ge(pe.h, wp)
                    for half in range(2):
                        pgn = pg_pair[half]
                        i = v.scalar_tensor_tensor(
                            x[oco][:, half * 1024:(half + 1) * 1024],
                            pgn[:], boc[:, oco:oco + 1],
                            x[oco][:, half * 1024:(half + 1) * 1024],
                            ALU.add, ALU.add)
                    dve.inc(i)
                wo_marks.append(dve.n)
            return dve.n

        @block.sync
        def _(sy):
            for hc in range(NHC):
                xl.inc(sy.dma_start(x[hc][:],
                                    d_x[0, hc * 128:(hc + 1) * 128, :]), 16)

        w_marks = {}
        first_kind = sub[0] if sub else "f"

        @block.gpsimd
        def _(g):
            if first_kind == "f":
                w_marks[("w1", 0)] = dl.inc(
                    g.dma_start(w_dst(w1s), w_src(d_w1[0])), 16)
                w_marks[("w2", 0)] = dl.inc(
                    g.dma_start(w_dst(w2s), w_src(d_w2[0])), 16)
            else:
                w_marks["wv"] = dl.inc(g.dma_start(w_dst(w1s), w_src(d_wv)), 16)
                w_marks["wk"] = dl.inc(g.dma_start(w_dst(w2s), w_src(d_wk)), 16)

        x_ready = [(xl.h, xl.n)]
        for s_i in range(n_seq):
            for sub_i, kind in enumerate(sub):
                last = (s_i, sub_i) == (n_seq - 1, len(sub) - 1)
                nxt_kind = None
                if not last:
                    nxt_kind = sub[sub_i + 1] if sub_i + 1 < len(sub) else sub[0]
                if kind == "f":
                    rearm = None
                    if nxt_kind == "a":
                        rearm = (d_wv, d_wk)
                    elif nxt_kind == "f":
                        rearm = (d_w1[0], d_w2[0])
                    done = emit_ffn(x_ready, w_marks, rearm=rearm)
                    if rearm is not None and nxt_kind == "a":
                        w_marks["wv"] = w_marks[("w1", 0)]
                        w_marks["wk"] = w_marks[("w2", 0)]
                    x_ready = [(dve.h, done)]
                elif kind == "a":
                    done = emit_attn(x_ready, w_marks)
                    x_ready = [(dve.h, done)]
                    if nxt_kind == "f":
                        @block.gpsimd
                        def _(g, wf=pe.n):
                            g.wait_ge(pe.h, wf)
                            w_marks[("w1", 0)] = dl.inc(
                                g.dma_start(w_dst(w1s), w_src(d_w1[0])), 16)
                            w_marks[("w2", 0)] = dl.inc(
                                g.dma_start(w_dst(w2s), w_src(d_w2[0])), 16)
                    elif nxt_kind == "a":
                        @block.gpsimd
                        def _(g, wf=pe.n):
                            g.wait_ge(pe.h, wf)
                            w_marks["wv"] = dl.inc(
                                g.dma_start(w_dst(w1s), w_src(d_wv)), 16)
                            w_marks["wk"] = dl.inc(
                                g.dma_start(w_dst(w2s), w_src(d_wk)), 16)

            @block.sync
            def _(sy, s_i=s_i, xr=x_ready):
                for sem, val in xr:
                    sy.wait_ge(sem, val)
                for hc in range(NHC):
                    st.inc(sy.dma_start(d_y[s_i, hc * 128:(hc + 1) * 128, :],
                                        x[hc][:]), 16)
                if s_i + 1 < n_seq:
                    sy.wait_ge(st.h, st.n)
                    for hc in range(NHC):
                        xl.inc(sy.dma_start(
                            x[hc][:],
                            d_x[s_i + 1, hc * 128:(hc + 1) * 128, :]), 16)
            if s_i + 1 < n_seq:
                x_ready = [(xl.h, xl.n)]

        @block.sync
        def _(sy):
            sy.wait_ge(st.h, st.n)

    return nc


def kernel(L0_attn_x, L0_attn_w_qkv, L0_attn_w_out, L0_attn_b_out,
           L0_attn_random_features, L0_ffn_Wg, L0_ffn_bg, L0_ffn_W1,
           L0_ffn_b1, L0_ffn_W2, L0_ffn_b2):
    n_seq = _CFG["n_seq"]
    B = L0_attn_x.shape[0]
    n_cores = B // n_seq
    xt = np.ascontiguousarray(np.transpose(np.asarray(L0_attn_x), (0, 2, 1)))

    wqkv = np.asarray(L0_attn_w_qkv)
    wqT = np.ascontiguousarray(wqkv[0:1024].T)
    wkT = np.ascontiguousarray(wqkv[1024:2048].T)
    wvT = np.ascontiguousarray(wqkv[2048:3072].T)
    woT = np.ascontiguousarray(np.asarray(L0_attn_w_out).T)
    rf = np.asarray(L0_attn_random_features)
    rfdup = np.ascontiguousarray(np.tile(rf.T, (2, 1)))
    b1 = np.asarray(L0_ffn_b1)
    b1c = np.ascontiguousarray(
        b1.reshape(E, 8, 128).transpose(2, 0, 1).reshape(128, 64))
    wgc = np.ascontiguousarray(
        np.asarray(L0_ffn_Wg).reshape(8, 128, 8).transpose(1, 0, 2).reshape(128, 64))
    bgc = np.ascontiguousarray(np.asarray(L0_ffn_bg).reshape(8, 1))
    boc = np.ascontiguousarray(np.asarray(L0_attn_b_out).reshape(8, 128).T)
    ident = np.eye(128, dtype=np.float32)
    io8 = np.tile(np.arange(8, dtype=np.float32), (128, 1))
    negsel = np.zeros((2, 128), np.float32)
    negsel[0, 0:64] = -1.0
    negsel[1, 64:128] = -1.0
    hselm = np.zeros((16, H), np.float32)
    for hh in range(16):
        hselm[hh, hh * 64:(hh + 1) * 64] = 1.0
    eselm = np.zeros((8, H), np.float32)
    for e in range(8):
        eselm[e, e * 128:(e + 1) * 128] = 1.0

    shared = {
        "W1": np.ascontiguousarray(np.asarray(L0_ffn_W1), dtype=np.float32),
        "W2": np.ascontiguousarray(np.asarray(L0_ffn_W2), dtype=np.float32),
        "b1c": b1c.astype(np.float32),
        "b2": np.ascontiguousarray(np.asarray(L0_ffn_b2), dtype=np.float32),
        "wgc": wgc.astype(np.float32), "bgc": bgc.astype(np.float32),
        "wqT": wqT.astype(np.float32), "wkT": wkT.astype(np.float32),
        "wvT": wvT.astype(np.float32), "woT": woT.astype(np.float32),
        "rfdup": rfdup.astype(np.float32), "boc": boc.astype(np.float32),
        "ident": ident, "iota8": io8, "negsel": negsel,
        "hsel": hselm, "esel": eselm,
    }
    nc = build_program(n_seq, _CFG["sub"])
    in_maps = []
    for c in range(n_cores):
        m = dict(shared)
        m["x_t"] = np.ascontiguousarray(xt[c * n_seq:(c + 1) * n_seq])
        in_maps.append(m)
    res = run_bass_kernel_spmd(nc, in_maps, core_ids=list(range(n_cores)))
    global _LAST
    _LAST = res.results
    outs = [r["y_t"] for r in res.results]
    y = np.concatenate(outs, axis=0)
    return np.ascontiguousarray(np.transpose(y, (0, 2, 1)))


# version 02
# speedup vs baseline: 1.0504x; 1.0504x over previous
import contextlib
import numpy as np
import concourse.bass as bass
import concourse.mybir as mybir
from concourse.bass_utils import run_bass_kernel_spmd

F32 = mybir.dt.float32
BF16 = mybir.dt.bfloat16
U32 = mybir.dt.uint32
AF = mybir.ActivationFunctionType
ALU = mybir.AluOpType

H = 1024
T = 2048
E = 8
HD = 64
NHC = H // 128

_CFG = {"n_seq": 2, "sub": ("f", "a", "f")}
_LAST = None


class Sem:
    def __init__(self, es, nc, name):
        self.h = es.enter_context(nc.semaphore(name))
        self.n = 0

    def inc(self, instr, k=1):
        instr.then_inc(self.h, k)
        self.n += k
        return self.n


def build_program(n_seq, sub):
    nc = bass.Bass("TRN2", target_bir_lowering=False)

    d_x = nc.dram_tensor("x_t", [n_seq, H, T], F32, kind="ExternalInput").ap()
    d_y = nc.dram_tensor("y_t", [n_seq, H, T], F32, kind="ExternalOutput").ap()
    d_w1 = nc.dram_tensor("W1", [E, H, H], F32, kind="ExternalInput").ap()
    d_w2 = nc.dram_tensor("W2", [E, H, H], F32, kind="ExternalInput").ap()
    d_b1c = nc.dram_tensor("b1c", [128, 64], F32, kind="ExternalInput").ap()
    d_b2 = nc.dram_tensor("b2", [E, H], F32, kind="ExternalInput").ap()
    d_wgc = nc.dram_tensor("wgc", [128, 64], F32, kind="ExternalInput").ap()
    d_bgc = nc.dram_tensor("bgc", [8, 1], F32, kind="ExternalInput").ap()
    d_wq = nc.dram_tensor("wqT", [H, H], F32, kind="ExternalInput").ap()
    d_wk = nc.dram_tensor("wkT", [H, H], F32, kind="ExternalInput").ap()
    d_wv = nc.dram_tensor("wvT", [H, H], F32, kind="ExternalInput").ap()
    d_wo = nc.dram_tensor("woT", [H, H], F32, kind="ExternalInput").ap()
    d_rf = nc.dram_tensor("rfdup", [128, HD], F32, kind="ExternalInput").ap()
    d_boc = nc.dram_tensor("boc", [128, 8], F32, kind="ExternalInput").ap()
    d_ident = nc.dram_tensor("ident", [128, 128], F32, kind="ExternalInput").ap()
    d_iota8 = nc.dram_tensor("iota8", [128, 8], F32, kind="ExternalInput").ap()
    d_negsel = nc.dram_tensor("negsel", [2, 128], F32, kind="ExternalInput").ap()
    d_hsel = nc.dram_tensor("hsel", [16, H], F32, kind="ExternalInput").ap()
    d_esel = nc.dram_tensor("esel", [8, H], F32, kind="ExternalInput").ap()

    dbg = _CFG.get("dbg", False)
    if dbg:
        d_dbg_xn = nc.dram_tensor("dbg_xn", [H, T], BF16, kind="ExternalOutput").ap()
        d_dbg_lg = nc.dram_tensor("dbg_lg", [8, T], F32, kind="ExternalOutput").ap()
        d_dbg_oh = nc.dram_tensor("dbg_oh", [8, T], BF16, kind="ExternalOutput").ap()
        d_dbg_h1 = nc.dram_tensor("dbg_h1", [128, 8192], BF16, kind="ExternalOutput").ap()
        d_dbg_mk = nc.dram_tensor("dbg_mk", [128, T], BF16, kind="ExternalOutput").ap()
        d_dbg_xe = nc.dram_tensor("dbg_xe", [E, H, T], F32, kind="ExternalOutput").ap()
        d_dbg_v = nc.dram_tensor("dbg_v", [128, 16384], BF16, kind="ExternalOutput").ap()
        d_dbg_kv = nc.dram_tensor("dbg_kv", [128, 520], BF16, kind="ExternalOutput").ap()
        d_dbg_ar = nc.dram_tensor("dbg_ar", [128, 16384], BF16, kind="ExternalOutput").ap()
        d_dbg_nr = nc.dram_tensor("dbg_nr", [16, T], BF16, kind="ExternalOutput").ap()
    es = contextlib.ExitStack()
    with es:
        sbt = lambda name, shape, dt: es.enter_context(nc.sbuf_tensor(name, shape, dt))
        pst = lambda name, shape: es.enter_context(nc.psum_tensor(name, shape, F32))

        x = [sbt(f"x{i}", [128, T], F32) for i in range(NHC)]
        xn = [sbt(f"xn{i}", [128, T], BF16) for i in range(NHC)]
        w1s = sbt("w1s", [128, 8192], BF16)
        w2s = sbt("w2s", [128, 8192], BF16)
        h1m = sbt("h1m", [128, 8192], BF16)
        vat = sbt("vat", [128, 16384], BF16)
        ar32 = sbt("ar32", [128, 3072], F32)
        ident = sbt("sb_ident", [128, 128], F32)
        iota8 = sbt("sb_iota8", [128, 8], F32)
        negsel = sbt("sb_negsel", [2, 128], BF16)
        hsel = sbt("sb_hsel", [16, H], BF16)
        esel = sbt("sb_esel", [8, H], BF16)
        rf_f = sbt("rf_f", [128, HD], F32)
        rf_s = sbt("rf_s", [128, HD], BF16)
        b1c = sbt("sb_b1c", [128, 64], F32)
        b2bf = sbt("b2bf", [8, H], BF16)
        wg = sbt("wg", [128, 64], BF16)
        bgc = sbt("sb_bgc", [8, 1], F32)
        boc = sbt("sb_boc", [128, 8], F32)
        ones_col = sbt("ones_col", [128, 1], BF16)
        ones_bc = sbt("ones_bc", [1, 128], BF16)
        bdiag = sbt("bdiag", [128, 2], BF16)
        eps_col = sbt("eps_col", [1, 1], F32)
        kvext = sbt("kvext", [128, 520], BF16)
        mx8 = sbt("mx8", [128, 8], F32)
        idxu = sbt("idxu", [128, 8], U32)
        idxf = sbt("idxf", [128, 1], F32)
        ohf = sbt("ohf", [128, 128], F32)
        ohstage = sbt("ohstage", [128, 8], F32)
        nrm16 = sbt("nrm16", [16, T], BF16)
        tmp_r = [sbt("tmp_r0", [1, 1024], BF16), sbt("tmp_r1", [1, 1024], BF16)]

        h1t = lambda ic: h1m[:, ic * 1024:(ic + 1) * 1024]
        xb_sl = [h1m[:, 0:2048], h1m[:, 2048:4096]]
        xsq_sl = [h1m[:, 4096:6144], h1m[:, 6144:8192]]
        kq_sb = h1m[:, 0:2048]
        sq_sb = h1m[:, 2048:4096]
        kp_sb = h1m[:, 4096:5120]
        nr_bf = h1m[0:2, 5120:7168]
        eq_sb = h1m[:, 2048:4096]
        mask_sl = [vat[:, 0:2048], vat[:, 2048:4096]]
        onehotT = vat[0:8, 4096:6144]
        m_bf = vat[0:1, 6144:7168]
        rstd_bf = vat[0:1, 7168:8192]
        vtile = lambda tt: vat[:, tt * 1024:(tt + 1) * 1024]
        araw = lambda oc: vat[:, oc * 2048:(oc + 1) * 2048]
        t1_sl = ar32[:, 0:1024]
        m_row = ar32[0:1, 1024:2048]
        pair_f = ar32[0:2, 1024:2048]
        msq_row = ar32[0:1, 2048:3072]
        var_row = ar32[0:1, 0:1024]
        rstd_f = ar32[0:1, 2048:3072]
        lg_sb = ar32[0:8, 0:2048]

        PG = [pst(f"PG{i}", [128, 1024]) for i in range(4)]

        dl = Sem(es, nc, "dl")
        im = Sem(es, nc, "im")
        xl = Sem(es, nc, "xl")
        pe = Sem(es, nc, "pe")
        act = Sem(es, nc, "act")
        dve = Sem(es, nc, "dve")
        sp = Sem(es, nc, "sp")
        st = Sem(es, nc, "st")

        block = es.enter_context(nc.Block())

        @block.gpsimd
        def _(g):
            for dst, src in [(ident[:], d_ident), (iota8[:], d_iota8),
                             (negsel[:], d_negsel), (hsel[:], d_hsel),
                             (esel[:], d_esel), (rf_f[:], d_rf),
                             (b1c[:], d_b1c), (b2bf[:], d_b2),
                             (wg[:], d_wgc), (bgc[:], d_bgc), (boc[:], d_boc)]:
                dl.inc(g.dma_start(dst, src), 16)
            im.inc(g.memset(ones_col[:], 1.0))
            im.inc(g.memset(ones_bc[:], 1.0))
            im.inc(g.memset(bdiag[:], 0.0))
            im.inc(g.memset(bdiag[0:64, 0:1], 1.0))
            im.inc(g.memset(bdiag[64:128, 1:2], 1.0))
            im.inc(g.memset(eps_col[:], 1e-5))
        init_dl = dl.n
        init_im = im.n

        @block.scalar
        def _(s):
            s.wait_ge(dl.h, init_dl)
            s.wait_ge(im.h, init_im)
            act.inc(s.activation(rf_s[:], rf_f[:], AF.Copy, scale=0.125))

        def vstep(v, instr):
            dve.inc(instr)
            v.wait_ge(dve.h, dve.n)

        def w_src(d_w):
            return d_w.rearrange("(c p) i -> p c i", p=128)

        def w_dst(slot):
            return slot[:].rearrange("p (c i) -> p c i", c=8)

        def emit_ln(x_ready):
            pe_base = pe.n
            pe_mark = lambda hc: pe_base + hc + 1
            act_base = act.n
            dve_base = dve.n

            @block.scalar
            def _(s):
                for sem, val in x_ready:
                    s.wait_ge(sem, val)
                for hc in range(NHC):
                    if hc >= 2:
                        s.wait_ge(pe.h, pe_mark(hc - 2))
                    act.inc(s.activation(xb_sl[hc % 2], x[hc][:], AF.Copy))
            assert act.n == act_base + NHC

            @block.vector
            def _(v):
                for sem, val in x_ready:
                    v.wait_ge(sem, val)
                for hc in range(NHC):
                    if hc >= 2:
                        v.wait_ge(pe.h, pe_mark(hc - 2))
                    dve.inc(v.tensor_mul(xsq_sl[hc % 2], x[hc][:], x[hc][:]))
            assert dve.n == dve_base + NHC

            @block.tensor
            def _(t):
                t.wait_ge(im.h, init_im)
                for hc in range(NHC):
                    t.wait_ge(act.h, act_base + hc + 1)
                    t.wait_ge(dve.h, dve_base + hc + 1)
                    first = hc == 0
                    last = hc == NHC - 1
                    sl = hc % 2
                    for half in range(2):
                        pgs, pgq = (PG[0], PG[1]) if half == 0 else (PG[2], PG[3])
                        for c in range(2):
                            co = half * 1024 + c * 512
                            t.matmul(pgs[0:1, c * 512:(c + 1) * 512],
                                     ones_col[:], xb_sl[sl][:, co:co + 512],
                                     start=first, stop=last)
                            i2 = t.matmul(pgq[0:1, c * 512:(c + 1) * 512],
                                          ones_col[:], xsq_sl[sl][:, co:co + 512],
                                          start=first, stop=last)
                    pe.inc(i2)
            sums_pe = pe.n
            assert sums_pe == pe_base + NHC

            xn_marks = [[0] * NHC, [0] * NHC]
            for half in range(2):
                pgs, pgq = (PG[0], PG[1]) if half == 0 else (PG[2], PG[3])

                @block.vector
                def _(v, pgs=pgs, pgq=pgq):
                    v.wait_ge(pe.h, sums_pe)
                    vstep(v, v.tensor_scalar_mul(m_row, pgs[0:1, :], 1.0 / H))
                    vstep(v, v.tensor_scalar_mul(msq_row, pgq[0:1, :], 1.0 / H))
                    vstep(v, v.tensor_mul(var_row, m_row, m_row))
                    dve.inc(v.tensor_sub(var_row, msq_row, var_row))
                v_rows = dve.n

                @block.scalar
                def _(s):
                    s.wait_ge(dve.h, v_rows)
                    act.inc(s.activation(rstd_f, var_row, AF.Ln,
                                         bias=eps_col[0:1, :]))
                    s.wait_ge(act.h, act.n)
                    act.inc(s.activation(rstd_f, rstd_f, AF.Exp, scale=-0.5))
                    s.wait_ge(act.h, act.n)
                    s.activation(m_bf[0:1, 0:1024], m_row, AF.Copy)
                    act.inc(s.activation(rstd_bf[0:1, 0:1024], rstd_f, AF.Copy))
                s_rows = act.n

                @block.tensor
                def _(t, pgs=pgs, pgq=pgq):
                    t.wait_ge(act.h, s_rows)
                    for c in range(2):
                        t.matmul(pgs[:, c * 512:(c + 1) * 512], ones_bc[:],
                                 m_bf[0:1, c * 512:(c + 1) * 512],
                                 start=True, stop=True)
                        i = t.matmul(pgq[:, c * 512:(c + 1) * 512], ones_bc[:],
                                     rstd_bf[0:1, c * 512:(c + 1) * 512],
                                     start=True, stop=True)
                    pe.inc(i)
                bc_pe = pe.n

                @block.vector
                def _(v, half=half, pgs=pgs, pgq=pgq):
                    v.wait_ge(pe.h, bc_pe)
                    co = half * 1024
                    for hc in range(NHC):
                        vstep(v, v.scalar_tensor_tensor(
                            t1_sl, x[hc][:, co:co + 1024], 1.0, pgs[:],
                            ALU.mult, ALU.subtract))
                        i = v.tensor_mul(xn[hc][:, co:co + 1024], t1_sl, pgq[:])
                        if hc < NHC - 1:
                            vstep(v, i)
                        else:
                            dve.inc(i)
                        xn_marks[half][hc] = dve.n
            return xn_marks

        def emit_gate(xn_marks):
            @block.tensor
            def _(t):
                t.wait_ge(dl.h, init_dl)
                t.wait_ge(im.h, init_im)
                for half in range(2):
                    pg = PG[half]
                    for hc in range(NHC):
                        t.wait_ge(dve.h, xn_marks[half][hc])
                        for c in range(2):
                            co = half * 1024 + c * 512
                            i = t.matmul(pg[0:8, c * 512:(c + 1) * 512],
                                         wg[:, hc * 8:hc * 8 + 8],
                                         xn[hc][:, co:co + 512],
                                         start=(hc == 0), stop=(hc == NHC - 1))
                pe.inc(i)
            g_pe = pe.n

            @block.scalar
            def _(s):
                s.wait_ge(pe.h, g_pe)
                for half in range(2):
                    i = s.activation(lg_sb[:, half * 1024:(half + 1) * 1024],
                                     PG[half][0:8, :], AF.Identity, bias=bgc[:])
                act.inc(i)
            lg_act = act.n

            @block.tensor
            def _(t):
                t.wait_ge(act.h, lg_act)
                for tt in range(16):
                    i = t.transpose(PG[2][:, tt * 8:tt * 8 + 8],
                                    lg_sb[:, tt * 128:(tt + 1) * 128],
                                    ident[0:8, 0:8])
                pe.inc(i)
            tr_pe = pe.n

            @block.vector
            def _(v):
                v.wait_ge(pe.h, tr_pe)
                for tt in range(16):
                    vstep(v, v.tensor_copy(ohstage[:], PG[2][:, tt * 8:tt * 8 + 8]))
                    vstep(v, v.max(mx8[:], ohstage[:]))
                    vstep(v, v.max_index(idxu[:], mx8[:], ohstage[:]))
                    vstep(v, v.tensor_copy(idxf[:], idxu[:, 0:1]))
                    i = v.tensor_scalar(ohf[:, tt * 8:tt * 8 + 8], iota8[:],
                                        idxf[:], None, ALU.is_equal)
                    if tt < 15:
                        vstep(v, i)
                dve.inc(i)
            oh_dve = dve.n

            pe_base = pe.n
            act_base = act.n

            @block.tensor
            def _(t):
                t.wait_ge(dve.h, oh_dve)
                for tt in range(16):
                    if tt >= 2:
                        t.wait_ge(act.h, act_base + tt - 1)
                    i = t.transpose(PG[3][0:8, (tt % 2) * 512:(tt % 2) * 512 + 128],
                                    ohf[:, tt * 8:tt * 8 + 8], ident[:])
                    pe.inc(i)

            @block.scalar
            def _(s):
                for tt in range(16):
                    s.wait_ge(pe.h, pe_base + tt + 1)
                    i = s.copy(onehotT[:, tt * 128:(tt + 1) * 128],
                               PG[3][0:8, (tt % 2) * 512:(tt % 2) * 512 + 128])
                    act.inc(i)
            return act.n

        def emit_mask(e_idx, slot, oh_act, war_act):
            @block.tensor
            def _(t):
                t.wait_ge(act.h, max(oh_act, war_act))
                for half in range(2):
                    for c in range(2):
                        co = half * 1024 + c * 512
                        i = t.matmul(PG[half][:, c * 512:(c + 1) * 512],
                                     esel[:, e_idx * 128:(e_idx + 1) * 128],
                                     onehotT[:, co:co + 512],
                                     start=True, stop=True)
                pe.inc(i)
            mpe = pe.n

            @block.scalar
            def _(s):
                s.wait_ge(pe.h, mpe)
                for half in range(2):
                    i = s.copy(mask_sl[slot][:, half * 1024:(half + 1) * 1024],
                               PG[half][:])
                act.inc(i)
            return act.n

        def emit_ffn(x_ready, w_marks, rearm):
            xn_marks = emit_ln(x_ready)
            oh_act = emit_gate(xn_marks)
            mask_acts = {0: emit_mask(0, 0, oh_act, war_act=oh_act)}
            last_ps1_act = mask_acts[0]
            prev_gelu = [0, 0]
            prev_acc = [0, 0]
            for e in range(E):
                for tb in range(2):
                    mm1_marks = []
                    act_base = act.n
                    gelu_mark = lambda ic: act_base + ic + 1
                    dve_base = dve.n
                    mask_mark = lambda ic: dve_base + ic + 1
                    for ic in range(8):
                        pg = PG[ic % 2]

                        @block.tensor
                        def _(t, e=e, tb=tb, ic=ic, pg=pg,
                              lpa=last_ps1_act, pga=prev_gelu[ic % 2]):
                            if tb == 0 and ic == 0:
                                t.wait_ge(dl.h, w_marks[("w1", e)])
                            if ic >= 2:
                                t.wait_ge(act.h, gelu_mark(ic - 2))
                            else:
                                t.wait_ge(act.h, max(lpa, pga))
                            for hc in range(NHC):
                                if e == 0:
                                    t.wait_ge(dve.h, xn_marks[tb][hc])
                                for c in range(2):
                                    i = t.matmul(
                                        pg[:, c * 512:(c + 1) * 512],
                                        w1s[:, hc * 1024 + ic * 128:
                                            hc * 1024 + (ic + 1) * 128],
                                        xn[hc][:, tb * 1024 + c * 512:
                                               tb * 1024 + (c + 1) * 512],
                                        start=(hc == 0), stop=(hc == NHC - 1))
                            pe.inc(i)
                        mm1_marks.append(pe.n)
                    if tb == 1:
                        w1_free_pe = pe.n

                    @block.scalar
                    def _(s, e=e, marks=tuple(mm1_marks)):
                        for ic in range(8):
                            s.wait_ge(pe.h, marks[ic])
                            act.inc(s.activation(
                                h1t(ic), PG[ic % 2][:], AF.Gelu,
                                bias=b1c[:, e * 8 + ic:e * 8 + ic + 1]))
                    assert act.n == act_base + 8
                    prev_gelu = [gelu_mark(6), gelu_mark(7)]

                    @block.vector
                    def _(v, e=e, tb=tb, ma=mask_acts[e]):
                        v.wait_ge(act.h, ma)
                        for ic in range(8):
                            v.wait_ge(act.h, gelu_mark(ic))
                            dve.inc(v.tensor_mul(
                                h1t(ic), h1t(ic),
                                mask_sl[e % 2][:, tb * 1024:(tb + 1) * 1024]))
                    assert dve.n == dve_base + 8

                    if tb == 1 and e + 1 < E:
                        mask_acts[e + 1] = emit_mask(
                            e + 1, (e + 1) % 2, oh_act, war_act=gelu_mark(7))
                        last_ps1_act = mask_acts[e + 1]

                    mm2_marks = []
                    acc_base = dve.n if (tb == 1 or e + 1 >= E) else dve.n
                    acc_base = dve.n
                    acc_mark = lambda hc: acc_base + hc + 1
                    for hc in range(NHC):
                        pg2 = PG[2 + hc % 2]

                        @block.tensor
                        def _(t, e=e, tb=tb, hc=hc, pg2=pg2,
                              md=mask_mark(7), pac=prev_acc[hc % 2]):
                            if tb == 0 and hc == 0:
                                t.wait_ge(dl.h, w_marks[("w2", e)])
                            t.wait_ge(dve.h, md)
                            if hc >= 2:
                                t.wait_ge(dve.h, acc_mark(hc - 2))
                            else:
                                t.wait_ge(dve.h, pac)
                            first = True
                            if e == 0:
                                for c in range(2):
                                    t.matmul(pg2[:, c * 512:(c + 1) * 512],
                                             b2bf[:, hc * 128:(hc + 1) * 128],
                                             onehotT[:, tb * 1024 + c * 512:
                                                     tb * 1024 + (c + 1) * 512],
                                             start=True, stop=False)
                                first = False
                            for ic in range(8):
                                for c in range(2):
                                    i = t.matmul(
                                        pg2[:, c * 512:(c + 1) * 512],
                                        w2s[:, ic * 1024 + hc * 128:
                                            ic * 1024 + (hc + 1) * 128],
                                        h1t(ic)[:, c * 512:(c + 1) * 512],
                                        start=(first and ic == 0),
                                        stop=(ic == 7))
                            pe.inc(i)
                        mm2_marks.append(pe.n)

                    @block.vector
                    def _(v, tb=tb, marks=tuple(mm2_marks)):
                        for hc in range(NHC):
                            v.wait_ge(pe.h, marks[hc])
                            dve.inc(v.scalar_tensor_tensor(
                                x[hc][:, tb * 1024:(tb + 1) * 1024],
                                PG[2 + hc % 2][:], 0.5,
                                x[hc][:, tb * 1024:(tb + 1) * 1024],
                                ALU.mult, ALU.add))
                    assert dve.n == acc_base + 8
                    prev_acc = [acc_mark(6), acc_mark(7)]

                if e + 1 < E:
                    @block.gpsimd
                    def _(g, e=e, wf=w1_free_pe, wf2=pe.n):
                        g.wait_ge(pe.h, wf)
                        w_marks[("w1", e + 1)] = dl.inc(
                            g.dma_start(w_dst(w1s), w_src(d_w1[e + 1])), 16)
                        g.wait_ge(pe.h, wf2)
                        w_marks[("w2", e + 1)] = dl.inc(
                            g.dma_start(w_dst(w2s), w_src(d_w2[e + 1])), 16)
            if rearm is not None:
                d1, d2 = rearm

                @block.gpsimd
                def _(g, wf=pe.n):
                    g.wait_ge(pe.h, wf)
                    w_marks[("w1", 0)] = dl.inc(
                        g.dma_start(w_dst(w1s), w_src(d1)), 16)
                    w_marks[("w2", 0)] = dl.inc(
                        g.dma_start(w_dst(w2s), w_src(d2)), 16)
            return dve.n

        def emit_attn(x_ready, w_marks):
            astage = _CFG.get("astage", 4)
            xn_marks = emit_ln(x_ready)
            xn_all = xn_marks[1][NHC - 1]

            v_acts = []
            for tt in range(16):
                pg = PG[tt % 2]

                @block.tensor
                def _(t, tt=tt, pg=pg):
                    if tt == 0:
                        t.wait_ge(dl.h, w_marks["wv"])
                    if tt >= 2:
                        t.wait_ge(act.h, v_acts[tt - 2])
                    half = tt // 8
                    for hc in range(NHC):
                        if tt % 8 == 0:
                            t.wait_ge(dve.h, xn_marks[half][hc])
                        for c in range(2):
                            i = t.matmul(
                                pg[:, c * 512:(c + 1) * 512],
                                xn[hc][:, tt * 128:(tt + 1) * 128],
                                w1s[:, hc * 1024 + c * 512:
                                    hc * 1024 + (c + 1) * 512],
                                start=(hc == 0), stop=(hc == NHC - 1))
                    pe.inc(i)
                v_pe = pe.n

                @block.scalar
                def _(s, tt=tt, pg=pg, vp=v_pe):
                    s.wait_ge(pe.h, vp)
                    act.inc(s.copy(vtile(tt), pg[:]))
                v_acts.append(act.n)
            v_done_pe = pe.n
            if dbg:
                @block.sync
                def _(sy, va=v_acts[-1]):
                    sy.wait_ge(act.h, va)
                    st.inc(sy.dma_start(d_dbg_v, vat[:]), 16)
            if astage <= 1:
                return xn_all

            @block.gpsimd
            def _(g, wf=v_done_pe):
                g.wait_ge(pe.h, wf)
                w_marks["wq"] = dl.inc(g.dma_start(w_dst(w1s), w_src(d_wq)), 16)

            prev_evac_act = v_acts[-1]
            prev_proj_pe = 0
            for oc in range(8):
                @block.tensor
                def _(t, oc=oc, pea=prev_evac_act):
                    if oc == 0:
                        t.wait_ge(dl.h, w_marks["wk"])
                    t.wait_ge(act.h, pea)
                    for hc in range(NHC):
                        for half in range(2):
                            for c in range(2):
                                i = t.matmul(
                                    PG[half][:, c * 512:(c + 1) * 512],
                                    w2s[:, hc * 1024 + oc * 128:
                                        hc * 1024 + (oc + 1) * 128],
                                    xn[hc][:, half * 1024 + c * 512:
                                           half * 1024 + (c + 1) * 512],
                                    start=(hc == 0), stop=(hc == NHC - 1))
                    pe.inc(i)
                qk_pe = pe.n
                if oc == 7:
                    @block.gpsimd
                    def _(g, wf=qk_pe):
                        g.wait_ge(pe.h, wf)
                        w_marks["wo"] = dl.inc(
                            g.dma_start(w_dst(w2s), w_src(d_wo)), 16)

                @block.scalar
                def _(s, qp=qk_pe, pp=prev_proj_pe):
                    s.wait_ge(pe.h, max(qp, pp))
                    s.copy(kq_sb[:, 0:1024], PG[0][:])
                    s.copy(kq_sb[:, 1024:2048], PG[1][:])
                    s.activation(sq_sb[:, 0:1024], PG[0][:], AF.Square)
                    i = s.activation(sq_sb[:, 1024:2048], PG[1][:], AF.Square)
                    act.inc(i)
                evac_act = act.n

                @block.tensor
                def _(t, ea=evac_act):
                    t.wait_ge(act.h, ea)
                    for half in range(2):
                        pgn = PG[2] if half == 0 else PG[3]
                        for c in range(2):
                            i = t.matmul(pgn[0:2, c * 512:(c + 1) * 512],
                                         bdiag[:],
                                         sq_sb[:, half * 1024 + c * 512:
                                               half * 1024 + (c + 1) * 512],
                                         start=True, stop=True)
                    pe.inc(i)
                nrm_pe = pe.n

                @block.scalar
                def _(s, np_=nrm_pe):
                    s.wait_ge(pe.h, np_)
                    for half in range(2):
                        pgn = PG[2] if half == 0 else PG[3]
                        act.inc(s.activation(pair_f, pgn[0:2, :], AF.Ln))
                        s.wait_ge(act.h, act.n)
                        i = s.activation(
                            nr_bf[0:2, half * 1024:(half + 1) * 1024],
                            pair_f, AF.Exp, scale=0.5)
                        if half == 0:
                            act.inc(i)
                            s.wait_ge(act.h, act.n)
                    act.inc(i)
                nr_act = act.n

                for hh in range(2):
                    @block.tensor
                    def _(t, hh=hh, na=nr_act, pea=prev_evac_act):
                        t.wait_ge(act.h, na)
                        t.wait_ge(act.h, pea)
                        for tt in range(16):
                            v_ = PG[2][:, tt * 64:(tt + 1) * 64]
                            t.matmul(v_,
                                     kq_sb[hh * 64:(hh + 1) * 64,
                                           tt * 128:(tt + 1) * 128],
                                     rf_s[hh * 64:(hh + 1) * 64, :],
                                     start=True, stop=False)
                            i = t.matmul(v_,
                                         nr_bf[0:2, tt * 128:(tt + 1) * 128],
                                         negsel[0:2, hh * 64:(hh + 1) * 64],
                                         start=False, stop=True)
                        pe.inc(i)
                    proj_pe = pe.n
                    prev_proj_pe = proj_pe

                    @block.scalar
                    def _(s, pp=proj_pe):
                        s.wait_ge(pe.h, pp)
                        act.inc(s.activation(kp_sb, PG[2][:], AF.Exp))
                    kp_act = act.n

                    @block.tensor
                    def _(t, oc=oc, hh=hh, ka=kp_act):
                        t.wait_ge(act.h, ka)
                        h_ = 2 * oc + hh
                        base = hh * 512
                        for tt in range(16):
                            t.matmul(PG[3][0:64, base:base + 64],
                                     kp_sb[:, tt * 64:(tt + 1) * 64],
                                     vtile(tt)[:, h_ * 64:h_ * 64 + 64],
                                     start=(tt == 0), stop=(tt == 15))
                            i = t.matmul(PG[2][0:64, 960 + hh:960 + hh + 1],
                                         kp_sb[:, tt * 64:(tt + 1) * 64],
                                         ones_col[:],
                                         start=(tt == 0), stop=(tt == 15))
                        pe.inc(i)
                    kv_pe = pe.n

                    @block.scalar
                    def _(s, oc=oc, hh=hh, kp=kv_pe, base=hh * 512):
                        s.wait_ge(pe.h, kp)
                        s.copy(kvext[hh * 64:(hh + 1) * 64,
                                     oc * 65:oc * 65 + 64],
                               PG[3][0:64, base:base + 64])
                        i = s.copy(kvext[hh * 64:(hh + 1) * 64,
                                         oc * 65 + 64:oc * 65 + 65],
                                   PG[2][0:64, 960 + hh:960 + hh + 1])
                        act.inc(i)
                    prev_evac_act = act.n

            if dbg:
                @block.sync
                def _(sy, ka=prev_evac_act):
                    sy.wait_ge(act.h, ka)
                    st.inc(sy.dma_start(d_dbg_kv, kvext[:]), 16)
            if astage <= 2:
                return xn_all

            prev_q_evac = prev_evac_act
            sp_marks = [0, 0]
            for oc in range(8):
                @block.tensor
                def _(t, oc=oc, pqe=prev_q_evac):
                    if oc == 0:
                        t.wait_ge(dl.h, w_marks["wq"])
                    t.wait_ge(act.h, pqe)
                    for hc in range(NHC):
                        for half in range(2):
                            for c in range(2):
                                i = t.matmul(
                                    PG[half][:, c * 512:(c + 1) * 512],
                                    w1s[:, hc * 1024 + oc * 128:
                                        hc * 1024 + (oc + 1) * 128],
                                    xn[hc][:, half * 1024 + c * 512:
                                           half * 1024 + (c + 1) * 512],
                                    start=(hc == 0), stop=(hc == NHC - 1))
                    pe.inc(i)
                qk_pe = pe.n

                @block.scalar
                def _(s, qp=qk_pe, pp=prev_proj_pe):
                    s.wait_ge(pe.h, max(qp, pp))
                    s.copy(kq_sb[:, 0:1024], PG[0][:])
                    s.copy(kq_sb[:, 1024:2048], PG[1][:])
                    s.activation(sq_sb[:, 0:1024], PG[0][:], AF.Square)
                    i = s.activation(sq_sb[:, 1024:2048], PG[1][:], AF.Square)
                    act.inc(i)
                evac_act = act.n

                @block.tensor
                def _(t, ea=evac_act):
                    t.wait_ge(act.h, ea)
                    for half in range(2):
                        pgn = PG[2] if half == 0 else PG[3]
                        for c in range(2):
                            i = t.matmul(pgn[0:2, c * 512:(c + 1) * 512],
                                         bdiag[:],
                                         sq_sb[:, half * 1024 + c * 512:
                                               half * 1024 + (c + 1) * 512],
                                         start=True, stop=True)
                    pe.inc(i)
                nrm_pe = pe.n

                @block.scalar
                def _(s, np_=nrm_pe):
                    s.wait_ge(pe.h, np_)
                    for half in range(2):
                        pgn = PG[2] if half == 0 else PG[3]
                        act.inc(s.activation(pair_f, pgn[0:2, :], AF.Ln))
                        s.wait_ge(act.h, act.n)
                        i = s.activation(
                            nr_bf[0:2, half * 1024:(half + 1) * 1024],
                            pair_f, AF.Exp, scale=0.5)
                        if half == 0:
                            act.inc(i)
                            s.wait_ge(act.h, act.n)
                    act.inc(i)
                nr_act = act.n

                @block.tensor
                def _(t, na=nr_act):
                    t.wait_ge(act.h, na)
                    for hh in range(2):
                        for half in range(2):
                            pgn = PG[2] if half == 0 else PG[3]
                            for c in range(2):
                                v_ = pgn[hh * 64:(hh + 1) * 64,
                                         c * 512:(c + 1) * 512]
                                t.matmul(v_, rf_s[hh * 64:(hh + 1) * 64, :],
                                         kq_sb[hh * 64:(hh + 1) * 64,
                                               half * 1024 + c * 512:
                                               half * 1024 + (c + 1) * 512],
                                         start=True, stop=False)
                                i = t.matmul(
                                    v_, negsel[0:2, hh * 64:(hh + 1) * 64],
                                    nr_bf[0:2, half * 1024 + c * 512:
                                          half * 1024 + (c + 1) * 512],
                                    start=False, stop=True)
                    pe.inc(i)
                proj_pe = pe.n
                prev_proj_pe = proj_pe

                @block.scalar
                def _(s, pp=proj_pe):
                    s.wait_ge(pe.h, pp)
                    s.activation(eq_sb[:, 0:1024], PG[2][:], AF.Exp)
                    i = s.activation(eq_sb[:, 1024:2048], PG[3][:], AF.Exp)
                    act.inc(i)
                eq_act = act.n

                for hh in range(2):
                    @block.tensor
                    def _(t, oc=oc, hh=hh, ea=eq_act, pqe=prev_q_evac):
                        t.wait_ge(act.h, ea)
                        t.wait_ge(act.h, pqe)
                        for half in range(2):
                            pgo = PG[0] if half == 0 else PG[1]
                            for c in range(2):
                                i = t.matmul(
                                    pgo[0:65, c * 512:(c + 1) * 512],
                                    kvext[hh * 64:(hh + 1) * 64,
                                          oc * 65:(oc + 1) * 65],
                                    eq_sb[hh * 64:(hh + 1) * 64,
                                          half * 1024 + c * 512:
                                          half * 1024 + (c + 1) * 512],
                                    start=True, stop=True)
                        pe.inc(i)
                    out_pe = pe.n

                    @block.scalar
                    def _(s, oc=oc, hh=hh, op=out_pe, spm=sp_marks[0]):
                        s.wait_ge(pe.h, op)
                        s.wait_ge(sp.h, spm)
                        for half in range(2):
                            pgo = PG[0] if half == 0 else PG[1]
                            s.copy(araw(oc)[hh * 64:(hh + 1) * 64,
                                            half * 1024:(half + 1) * 1024],
                                   pgo[0:64, :])
                            i = s.copy(tmp_r[half][0:1, :], pgo[64:65, :])
                        act.inc(i)
                    ev_act = act.n
                    prev_q_evac = ev_act

                    @block.sync
                    def _(sy, oc=oc, hh=hh, ea=ev_act):
                        sy.wait_ge(act.h, ea)
                        row = 2 * oc + hh
                        sp.inc(sy.dma_start(
                            nrm16[row:row + 1, 0:1024], tmp_r[0][:]), 16)
                        sp.inc(sy.dma_start(
                            nrm16[row:row + 1, 1024:2048], tmp_r[1][:]), 16)
                    sp_marks[0] = sp.n

            if dbg:
                @block.sync
                def _(sy, qa=prev_q_evac, spn=sp.n):
                    sy.wait_ge(act.h, qa)
                    sy.wait_ge(sp.h, spn)
                    st.inc(sy.dma_start(d_dbg_ar, vat[:]), 16)
                    st.inc(sy.dma_start(d_dbg_nr, nrm16[:]), 16)
            if astage <= 3:
                return xn_all

            @block.vector
            def _(v, spn=sp.n):
                v.wait_ge(sp.h, spn)
                vstep(v, v.tensor_scalar_add(nrm16[:], nrm16[:], 1e-6))
                with nc.allow_low_precision(
                        reason="1/(nrm+1e-6): fp32 DVE internals, bf16 store "
                               "at matmul-operand precision"):
                    dve.inc(v.reciprocal(nrm16[:], nrm16[:]))
            al_dve = dve.n

            sc_dves = []
            for oc in range(8):
                pg_sc = (PG[2], PG[3]) if oc % 2 == 0 else (PG[0], PG[1])

                @block.tensor
                def _(t, oc=oc, ad=al_dve, pqe=prev_q_evac, pg_sc=pg_sc):
                    if oc == 0:
                        t.wait_ge(dve.h, ad)
                    if oc <= 1:
                        t.wait_ge(act.h, pqe)
                    if oc >= 2:
                        t.wait_ge(dve.h, sc_dves[oc - 2])
                    for half in range(2):
                        pgn = pg_sc[half]
                        for c in range(2):
                            i = t.matmul(pgn[:, c * 512:(c + 1) * 512],
                                         hsel[:, oc * 128:(oc + 1) * 128],
                                         nrm16[:, half * 1024 + c * 512:
                                               half * 1024 + (c + 1) * 512],
                                         start=True, stop=True)
                    pe.inc(i)
                hs_pe = pe.n

                @block.vector
                def _(v, oc=oc, hp=hs_pe, pg_sc=pg_sc):
                    v.wait_ge(pe.h, hp)
                    for half in range(2):
                        pgn = pg_sc[half]
                        i = v.tensor_mul(
                            araw(oc)[:, half * 1024:(half + 1) * 1024],
                            araw(oc)[:, half * 1024:(half + 1) * 1024],
                            pgn[:])
                    dve.inc(i)
                sc_dves.append(dve.n)

            wo_marks = []
            for oco in range(8):
                pg_pair = (PG[0], PG[1]) if oco % 2 == 0 else (PG[2], PG[3])

                @block.tensor
                def _(t, oco=oco, pg_pair=pg_pair, pqe=prev_q_evac):
                    if oco == 0:
                        t.wait_ge(dl.h, w_marks["wo"])
                        t.wait_ge(dve.h, sc_dves[-1])
                        t.wait_ge(act.h, pqe)
                    if oco >= 2:
                        t.wait_ge(dve.h, wo_marks[oco - 2])
                    for hc in range(NHC):
                        for half in range(2):
                            pgn = pg_pair[half]
                            for c in range(2):
                                i = t.matmul(
                                    pgn[:, c * 512:(c + 1) * 512],
                                    w2s[:, hc * 1024 + oco * 128:
                                        hc * 1024 + (oco + 1) * 128],
                                    araw(hc)[:, half * 1024 + c * 512:
                                             half * 1024 + (c + 1) * 512],
                                    start=(hc == 0), stop=(hc == NHC - 1))
                    pe.inc(i)
                wo_pe = pe.n

                @block.vector
                def _(v, oco=oco, wp=wo_pe, pg_pair=pg_pair):
                    v.wait_ge(pe.h, wp)
                    for half in range(2):
                        pgn = pg_pair[half]
                        i = v.scalar_tensor_tensor(
                            x[oco][:, half * 1024:(half + 1) * 1024],
                            pgn[:], boc[:, oco:oco + 1],
                            x[oco][:, half * 1024:(half + 1) * 1024],
                            ALU.add, ALU.add)
                    dve.inc(i)
                wo_marks.append(dve.n)
            return dve.n

        @block.sync
        def _(sy):
            for hc in range(NHC):
                xl.inc(sy.dma_start(x[hc][:],
                                    d_x[0, hc * 128:(hc + 1) * 128, :]), 16)

        w_marks = {}
        first_kind = sub[0] if sub else "f"

        @block.gpsimd
        def _(g):
            if first_kind == "f":
                w_marks[("w1", 0)] = dl.inc(
                    g.dma_start(w_dst(w1s), w_src(d_w1[0])), 16)
                w_marks[("w2", 0)] = dl.inc(
                    g.dma_start(w_dst(w2s), w_src(d_w2[0])), 16)
            else:
                w_marks["wv"] = dl.inc(g.dma_start(w_dst(w1s), w_src(d_wv)), 16)
                w_marks["wk"] = dl.inc(g.dma_start(w_dst(w2s), w_src(d_wk)), 16)

        x_ready = [(xl.h, xl.n)]
        for s_i in range(n_seq):
            for sub_i, kind in enumerate(sub):
                last = (s_i, sub_i) == (n_seq - 1, len(sub) - 1)
                nxt_kind = None
                if not last:
                    nxt_kind = sub[sub_i + 1] if sub_i + 1 < len(sub) else sub[0]
                if kind == "f":
                    rearm = None
                    if nxt_kind == "a":
                        rearm = (d_wv, d_wk)
                    elif nxt_kind == "f":
                        rearm = (d_w1[0], d_w2[0])
                    done = emit_ffn(x_ready, w_marks, rearm=rearm)
                    if rearm is not None and nxt_kind == "a":
                        w_marks["wv"] = w_marks[("w1", 0)]
                        w_marks["wk"] = w_marks[("w2", 0)]
                    x_ready = [(dve.h, done)]
                elif kind == "a":
                    done = emit_attn(x_ready, w_marks)
                    x_ready = [(dve.h, done)]
                    if nxt_kind == "f":
                        @block.gpsimd
                        def _(g, wf=pe.n):
                            g.wait_ge(pe.h, wf)
                            w_marks[("w1", 0)] = dl.inc(
                                g.dma_start(w_dst(w1s), w_src(d_w1[0])), 16)
                            w_marks[("w2", 0)] = dl.inc(
                                g.dma_start(w_dst(w2s), w_src(d_w2[0])), 16)
                    elif nxt_kind == "a":
                        @block.gpsimd
                        def _(g, wf=pe.n):
                            g.wait_ge(pe.h, wf)
                            w_marks["wv"] = dl.inc(
                                g.dma_start(w_dst(w1s), w_src(d_wv)), 16)
                            w_marks["wk"] = dl.inc(
                                g.dma_start(w_dst(w2s), w_src(d_wk)), 16)

            @block.sync
            def _(sy, s_i=s_i, xr=x_ready):
                for sem, val in xr:
                    sy.wait_ge(sem, val)
                for hc in range(NHC):
                    st.inc(sy.dma_start(d_y[s_i, hc * 128:(hc + 1) * 128, :],
                                        x[hc][:]), 16)
                if s_i + 1 < n_seq:
                    sy.wait_ge(st.h, st.n)
                    for hc in range(NHC):
                        xl.inc(sy.dma_start(
                            x[hc][:],
                            d_x[s_i + 1, hc * 128:(hc + 1) * 128, :]), 16)
            if s_i + 1 < n_seq:
                x_ready = [(xl.h, xl.n)]

        @block.sync
        def _(sy):
            sy.wait_ge(st.h, st.n)

    return nc


def kernel(L0_attn_x, L0_attn_w_qkv, L0_attn_w_out, L0_attn_b_out,
           L0_attn_random_features, L0_ffn_Wg, L0_ffn_bg, L0_ffn_W1,
           L0_ffn_b1, L0_ffn_W2, L0_ffn_b2):
    n_seq = _CFG["n_seq"]
    B = L0_attn_x.shape[0]
    n_cores = B // n_seq
    xt = np.ascontiguousarray(np.transpose(np.asarray(L0_attn_x), (0, 2, 1)))

    wqkv = np.asarray(L0_attn_w_qkv)
    wqT = np.ascontiguousarray(wqkv[0:1024].T)
    wkT = np.ascontiguousarray(wqkv[1024:2048].T)
    wvT = np.ascontiguousarray(wqkv[2048:3072].T)
    woT = np.ascontiguousarray(np.asarray(L0_attn_w_out).T)
    rf = np.asarray(L0_attn_random_features)
    rfdup = np.ascontiguousarray(np.tile(rf.T, (2, 1)))
    b1 = np.asarray(L0_ffn_b1)
    b1c = np.ascontiguousarray(
        b1.reshape(E, 8, 128).transpose(2, 0, 1).reshape(128, 64))
    wgc = np.ascontiguousarray(
        np.asarray(L0_ffn_Wg).reshape(8, 128, 8).transpose(1, 0, 2).reshape(128, 64))
    bgc = np.ascontiguousarray(np.asarray(L0_ffn_bg).reshape(8, 1))
    boc = np.ascontiguousarray(np.asarray(L0_attn_b_out).reshape(8, 128).T)
    ident = np.eye(128, dtype=np.float32)
    io8 = np.tile(np.arange(8, dtype=np.float32), (128, 1))
    negsel = np.zeros((2, 128), np.float32)
    negsel[0, 0:64] = -1.0
    negsel[1, 64:128] = -1.0
    hselm = np.zeros((16, H), np.float32)
    for hh in range(16):
        hselm[hh, hh * 64:(hh + 1) * 64] = 1.0
    eselm = np.zeros((8, H), np.float32)
    for e in range(8):
        eselm[e, e * 128:(e + 1) * 128] = 1.0

    shared = {
        "W1": np.ascontiguousarray(np.asarray(L0_ffn_W1), dtype=np.float32),
        "W2": np.ascontiguousarray(np.asarray(L0_ffn_W2), dtype=np.float32),
        "b1c": b1c.astype(np.float32),
        "b2": np.ascontiguousarray(np.asarray(L0_ffn_b2), dtype=np.float32),
        "wgc": wgc.astype(np.float32), "bgc": bgc.astype(np.float32),
        "wqT": wqT.astype(np.float32), "wkT": wkT.astype(np.float32),
        "wvT": wvT.astype(np.float32), "woT": woT.astype(np.float32),
        "rfdup": rfdup.astype(np.float32), "boc": boc.astype(np.float32),
        "ident": ident, "iota8": io8, "negsel": negsel,
        "hsel": hselm, "esel": eselm,
    }
    nc = build_program(n_seq, _CFG["sub"])
    in_maps = []
    for c in range(n_cores):
        m = dict(shared)
        m["x_t"] = np.ascontiguousarray(xt[c * n_seq:(c + 1) * n_seq])
        in_maps.append(m)
    res = run_bass_kernel_spmd(nc, in_maps, core_ids=list(range(n_cores)))
    global _LAST
    _LAST = res.results
    outs = [r["y_t"] for r in res.results]
    y = np.concatenate(outs, axis=0)
    return np.ascontiguousarray(np.transpose(y, (0, 2, 1)))
